# version 06
# baseline (speedup 1.0000x reference)
import numpy as np
import concourse.bass as bass
import concourse.mybir as mybir
from concourse.bass_utils import run_bass_kernel_spmd

F32 = mybir.dt.float32
BF16 = mybir.dt.bfloat16
AF = mybir.ActivationFunctionType
ALU = mybir.AluOpType
AX = mybir.AxisListType

MASK_NEG = -66504.0
SOFT_C = 24.0
LN_EPS = 1e-5

CFG_FULL = dict(S=4096, E=2048, INTER=1408, NEXP=16, SHI=2816, n_cores=8)


class Plan:

    ENGS = ("sync", "gpsimd", "tensor", "vector", "scalar")

    def __init__(self):
        self.q = {e: [] for e in self.ENGS}
        self.c = {}
        self.hwm = {e: {} for e in self.ENGS}
        self.pend = {e: None for e in self.ENGS}

    def inc(self, sem, n=1):
        self.c[sem] = self.c.get(sem, 0) + n
        return self.c[sem]

    def tick(self, sem, n=1):
        self.c[sem] = self.c.get(sem, 0) + n
        return (sem, n)

    def at(self, sem):
        return (sem, self.c.get(sem, 0))

    AUTOSEM = {"vector": "dv", "scalar": "ac", "gpsimd": "po"}

    def barrier(self):
        for e in self.ENGS:
            self.inc("bar", 1)
            self.op(e, lambda E_: E_.sem_inc(self._bar, 1), is_dma=True)
        tgt = self.at("bar")
        for e in self.ENGS:
            self.pend[e] = tgt

    def op(self, eng, fn, wait=(), inc=(), is_dma=False):
        a = None if is_dma else self.AUTOSEM.get(eng)
        if a is not None:
            inc = tuple(inc) + (self.tick(a),)
        if self.pend.get(eng) is not None:
            wait = list(wait) + [self.pend[eng]]
            self.pend[eng] = None
        w = []
        def add(it):
            if it is None:
                return
            if isinstance(it, list):
                for j in it:
                    add(j)
                return
            s, v = it
            if v <= 0 or v <= self.hwm[eng].get(s, 0):
                return
            self.hwm[eng][s] = v
            w.append((s, v))
        for it in wait:
            add(it)
        self.q[eng].append((tuple(w), fn, tuple(inc)))

    def dma(self, eng, sem, dst, src, wait=()):
        v = self.inc(sem, 16)
        self.op(eng, lambda E, d=dst, s=src: E.dma_start(d, s),
                wait=wait, inc=((sem, 16),), is_dma=True)
        return (sem, v)


def build(cfg, debug=False, race_check=True, phases="0123", h_lim=None, qg_lim=None, p1_parts="wqva", dbg_attn=None):
    S, E = cfg["S"], cfg["E"]
    INTER, NEXP, SHI = cfg["INTER"], cfg["NEXP"], cfg["SHI"]
    H = E // 128
    EC = E // 128
    IC = INTER // 128
    SIC = SHI // 128
    NT = S // 128
    NQG = S // 512
    NQ = E // 512
    assert EC % 4 == 0 and SIC == 2 * IC and S % 512 == 0

    nc = bass.Bass("TRN2", target_bir_lowering=False,
                   detect_race_conditions=race_check)

    x_in = nc.dram_tensor("x", [S, E], F32, kind="ExternalInput").ap()
    qw_in = nc.dram_tensor("qw", [E, E], F32, kind="ExternalInput").ap()
    kw_in = nc.dram_tensor("kw", [E, E], F32, kind="ExternalInput").ap()
    vw_in = nc.dram_tensor("vw", [E, E], F32, kind="ExternalInput").ap()
    ow_in = nc.dram_tensor("ow", [E, E], F32, kind="ExternalInput").ap()
    gw_in = nc.dram_tensor("gw", [E, NEXP], F32, kind="ExternalInput").ap()
    w1t_in = nc.dram_tensor("w1t", [NEXP, E, INTER], F32, kind="ExternalInput").ap()
    w3t_in = nc.dram_tensor("w3t", [NEXP, E, INTER], F32, kind="ExternalInput").ap()
    w2t_in = nc.dram_tensor("w2t", [NEXP, INTER, E], F32, kind="ExternalInput").ap()
    sw1t_in = nc.dram_tensor("sw1t", [E, SHI], F32, kind="ExternalInput").ap()
    sw3t_in = nc.dram_tensor("sw3t", [E, SHI], F32, kind="ExternalInput").ap()
    sw2t_in = nc.dram_tensor("sw2t", [SHI, E], F32, kind="ExternalInput").ap()
    cst_in = nc.dram_tensor("cst", [128, 512], F32, kind="ExternalInput").ap()
    out_t = nc.dram_tensor("out", [S, E], F32, kind="ExternalOutput").ap()

    scratch_kind = "ExternalOutput" if debug else "Internal"
    qwr = nc.dram_tensor("qwr", [H, 128, EC, 128], BF16, kind="Internal").ap()
    kwr = nc.dram_tensor("kwr", [H, 128, EC, 128], BF16, kind="Internal").ap()
    vwr = nc.dram_tensor("vwr", [H, 128, EC, 128], BF16, kind="Internal").ap()
    owb = nc.dram_tensor("owb", [E, E], BF16, kind="Internal").ap()
    w1r = nc.dram_tensor("w1r", [NEXP, IC, 128, EC, 128], BF16, kind="Internal").ap()
    w3r = nc.dram_tensor("w3r", [NEXP, IC, 128, EC, 128], BF16, kind="Internal").ap()
    w2b = nc.dram_tensor("w2b", [NEXP, INTER, E], BF16, kind="Internal").ap()
    sw1r = nc.dram_tensor("sw1r", [SIC, 128, EC, 128], BF16, kind="Internal").ap()
    sw3r = nc.dram_tensor("sw3r", [SIC, 128, EC, 128], BF16, kind="Internal").ap()
    sw2b = nc.dram_tensor("sw2b", [SHI, E], BF16, kind="Internal").ap()
    ots = nc.dram_tensor("ots", [H, 128, S], BF16, kind=scratch_kind).ap()
    x1s = nc.dram_tensor("x1s", [S, E], F32, kind=scratch_kind).ap()
    tts = nc.dram_tensor("tts", [EC, 128, S], BF16, kind=scratch_kind).ap()

    def sb(name, cols, dtype):
        return nc.alloc_sbuf_tensor(name, [128, cols], dtype)

    cst = sb("cst_sb", 512, F32)
    id16 = sb("id16", 128, BF16)
    ones16 = sb("ones16", 128, BF16)
    wg = sb("wg", NT * NEXP, F32)
    aux = [sb(f"aux{i}", 64, F32) for i in range(2)]
    gaux = sb("gaux", 16, F32)
    ge_sb = sb("ge_sb", NEXP, F32)
    gp_sb = sb("gp_sb", NEXP, F32)
    gi_sb = sb("gi_sb", NEXP, F32)
    gj_sb = sb("gj_sb", NEXP, F32)
    gw_sb = sb("gw_sb", EC * NEXP, F32)
    ptb = [sb(f"ptb{i}", 512, BF16) for i in range(2)]
    invs = [sb(f"invs{i}", 512, F32) for i in range(2)]
    otst = [sb(f"otst{i}", 512, BF16) for i in range(2)]
    sa_st = sb("sa_st", 512, BF16)
    sb_st = sb("sb_st", 512, BF16)
    zflush = sb("zflush", 512, BF16)

    A_BYTES = max(2 * EC * S, EC * (3072 + 2 * E),
                  EC * 1024 + 16 * E + IC * 1024 + 20 * E)
    arena = sb("arena", A_BYTES, mybir.dt.uint8)
    a_base = nc.lookup_mloc(arena).addr

    def at(name, cols, dtype, off_bytes):
        itm = 2 if dtype == BF16 else 4
        assert off_bytes + cols * itm <= A_BYTES, (name, off_bytes, cols)
        return nc.alloc_sbuf_tensor_at(name, [128, cols], dtype,
                                       offset=a_base + off_bytes)

    xn1T = at("xn1T", EC * S, BF16, 0)
    otsb = [at(f"otsb{i}", EC * 512, BF16, i * EC * 1024) for i in range(2)]
    owsb = at("owsb", EC * E, BF16, 2 * EC * 1024)
    ttst = at("ttst", EC * 512, BF16, 2 * EC * 1024 + 2 * EC * E)
    ttsb = at("ttsb", EC * 512, BF16, 0)
    yacc = at("yacc", 4 * E, F32, EC * 1024)
    hT = at("hT", IC * 512, BF16, EC * 1024 + 16 * E)
    _o3 = EC * 1024 + 16 * E + IC * 1024
    x1ld = [at(f"x1ld{i}", E, F32, _o3 + i * 4 * E) for i in range(2)]
    outst = [at(f"outst{i}", E, F32, _o3 + (2 + i) * 4 * E) for i in range(2)]
    epiln = at("epiln", E, F32, _o3 + 4 * 4 * E)

    B_BYTES = 57344
    arenb = sb("arenb", B_BYTES, mybir.dt.uint8)
    b_base = nc.lookup_mloc(arenb).addr

    def bt(name, cols, dtype, off_bytes):
        itm = 2 if dtype == BF16 else 4
        assert off_bytes + cols * itm <= B_BYTES, (name, off_bytes, cols)
        return nc.alloc_sbuf_tensor_at(name, [128, cols], dtype,
                                       offset=b_base + off_bytes)

    xld = [bt(f"xld{i}", E, F32, i * 4 * E) for i in range(2)]
    xn1 = [bt(f"xn1{i}", E, F32, (2 + i) * 4 * E) for i in range(2)]
    qt_h = bt("qt_h", S, BF16, 0)
    kt_h = bt("kt_h", S, BF16, 2 * S)
    vt_h = bt("vt_h", S, BF16, 4 * S)
    v_sb = bt("v_sb", S, BF16, 6 * S)
    p1w = 8 * S
    qwsb = [bt(f"qwsb{i}", E, BF16, p1w + i * 2 * E) for i in range(2)]
    kwsb = [bt(f"kwsb{i}", E, BF16, p1w + (2 + i) * 2 * E) for i in range(2)]
    vwsb = [bt(f"vwsb{i}", E, BF16, p1w + (4 + i) * 2 * E) for i in range(2)]
    assert p1w + 6 * 2 * E <= B_BYTES
    xld2 = [bt(f"xld2{i}", E, F32, i * 4 * E) for i in range(2)]
    uln = bt("uln", E, F32, 2 * 4 * E)
    x1st = [bt(f"x1st{i}", E, F32, (3 + i) * 4 * E) for i in range(2)]
    t32 = bt("t32", E, F32, 5 * 4 * E)
    t32T = bt("t32T", E, F32, 6 * 4 * E)
    w1sb = [bt(f"w1sb{i}", EC * 128, BF16, i * EC * 256) for i in range(2)]
    w3sb = [bt(f"w3sb{i}", EC * 128, BF16, (2 + i) * EC * 256) for i in range(2)]
    w2q = [bt(f"w2q{i}", IC * 512, BF16, 4 * EC * 256 + i * IC * 1024)
           for i in range(2)]
    assert 4 * EC * 256 + 2 * IC * 1024 <= B_BYTES

    pu = nc.alloc_psum_tensor("pu", [128, 2048], F32)
    ps4 = nc.alloc_psum_tensor("ps4", [128, 512], F32)
    ps5 = nc.alloc_psum_tensor("ps5", [128, 512], F32)
    psb = nc.alloc_psum_tensor("psb", [128, 2048], BF16)

    def puq(i):
        return pu[:, i * 512:(i + 1) * 512]

    c_id = cst[:, 0:128]
    c_mask = cst[:, 128:256]
    c_ones = cst[:, 256:384]
    c_eps = cst[:, 384:385]
    c_mhalf = cst[:, 385:386]
    c_isd = cst[:, 386:387]
    c_negC = cst[:, 387:388]

    P = Plan()
    bankfree = {}
    buffree = {}

    def free(d, key, sem):
        d[key] = P.at(sem)

    def wfree(d, key):
        return d.get(key)

    def ln_chain(a, srcs, mcol=24, wait=(), scol=0):
        for q2, s2 in enumerate(srcs):
            P.op("vector", lambda E_, s2=s2, q2=q2, a=a: E_.bn_stats(
                a[:, scol + q2 * 6:scol + q2 * 6 + 6], s2),
                 wait=(wait if q2 == 0 else ()))
        P.op("vector", lambda E_, a=a: E_.bn_aggr(
            a[:, mcol:mcol + 2], a[:, scol:scol + 6 * NQ]),
             wait=[P.at("dv")])
        P.op("scalar", lambda E_, a=a: E_.activation(
            a[:, mcol + 2:mcol + 3], a[:, mcol + 1:mcol + 2], AF.Ln, bias=c_eps),
             wait=[P.at("dv")])
        P.op("scalar", lambda E_, a=a: E_.activation(
            a[:, mcol + 3:mcol + 4], a[:, mcol + 2:mcol + 3], AF.Exp, scale=c_mhalf),
             wait=[P.at("ac")])
        P.op("vector", lambda E_, a=a: E_.scalar_tensor_tensor(
            a[:, mcol + 4:mcol + 5], a[:, mcol:mcol + 1], -1.0,
            a[:, mcol + 3:mcol + 4], op0=ALU.mult, op1=ALU.mult),
             wait=[P.at("ac")])
        return P.at("dv")

    ld_cst = P.dma("sync", "cst", cst[:], cst_in)
    P.op("gpsimd", lambda E_: E_.tensor_copy(id16[:], c_id), wait=[ld_cst])
    P.op("gpsimd", lambda E_: E_.tensor_copy(ones16[:], c_ones))
    P.op("gpsimd", lambda E_: E_.memset(zflush[:], 0))
    consts_ready = P.at("po")

    def pe_flush(region, wait=()):
        n = region.ap[-1][1]
        P.op("tensor", lambda E_, r=region, n=n: E_.matmul(
            r, ones16[:, 0:1], zflush[:, 0:n], start=False, stop=True,
            skip_group_check=True),
             wait=[consts_ready] + list(wait), inc=[P.tick("pe")])
        return P.at("pe")

    def retile_sq(dst, src_m, ic_count, sem):
        r = src_m.rearrange("(dc p) (ic i) -> ic p dc i", p=128, i=128)
        m = None
        for ic in range(ic_count):
            m = P.dma("gpsimd", sem, dst[ic], r[ic])
        return m

    for h in range(H):
        for dst, src in ((qwr, qw_in), (kwr, kw_in), (vwr, vw_in)):
            r = src.rearrange("(c p) (h j) -> h p c j", p=128, j=128)
            P.dma("gpsimd", "rqk", dst[h], r[h])
    P.dma("gpsimd", "rqk", owb, ow_in)
    attn_rt = ow_rt = P.at("rqk")
    exp_rt = []
    for e in range(NEXP):
        sem = f"re{e}"
        retile_sq(w1r[e], w1t_in[e], IC, sem)
        retile_sq(w3r[e], w3t_in[e], IC, sem)
        P.dma("gpsimd", sem, w2b[e], w2t_in[e])
        exp_rt.append(P.at(sem))
    retile_sq(sw1r, sw1t_in, SIC, "rsh")
    retile_sq(sw3r, sw3t_in, SIC, "rsh")
    P.dma("gpsimd", "rsh", sw2b, sw2t_in)
    sh_rt = P.at("rsh")

    for t in range(NT):
        par = t % 2
        ldx = P.dma("sync", ("xa", "xb")[par], xld[par][:],
                    x_in[t * 128:(t + 1) * 128, :],
                    wait=[wfree(buffree, ("xld", par))])

        a = aux[par]
        ln_chain(a, [xld[par][:, q2 * 512:(q2 + 1) * 512] for q2 in range(NQ)],
                 wait=[ldx, wfree(buffree, ("aux", par))])
        P.op("scalar", lambda E_, s=xld[par], d=xn1[par], a=a: E_.activation(
            d[:], s[:], AF.Identity, scale=a[:, 27:28], bias=a[:, 28:29]),
             wait=[P.at("dv"), wfree(buffree, ("xn1", par))])
        apply_ms = P.at("ac")
        free(buffree, ("xld", par), "ac")
        free(buffree, ("aux", par), "ac")

        for g4 in range(EC // 4):
            bank = (ps4, ps5)[(t * (EC // 4) + g4) % 2]
            bkey = ("ps4", "ps5")[(t * (EC // 4) + g4) % 2]
            for k in range(4):
                c = g4 * 4 + k
                P.op("tensor", lambda E_, b=bank, k=k, c=c, s=xn1[par]: E_.transpose(
                    b[:, k * 128:(k + 1) * 128], s[:, c * 128:(c + 1) * 128], c_id),
                     wait=([apply_ms, ld_cst, wfree(bankfree, bkey)] if k == 0 else []),
                     inc=([P.tick("pe")] if k == 3 else []))
            pe_flush(bank[0:1, 384:512])
            tp_ms = pe_flush(bank[0:1, 384:512])
            def evac(E_, b=bank, g4=g4, t=t):
                src = b.ap().rearrange("p (c k) -> p c k", k=128)[:, 0:4, :]
                dst = xn1T.ap().rearrange("p (c s) -> p c s", c=EC)[
                    :, g4 * 4:g4 * 4 + 4, t * 128:(t + 1) * 128]
                return E_.tensor_copy(dst, src)
            P.op("vector", evac, wait=[tp_ms])
            free(bankfree, bkey, "dv")
        free(buffree, ("xn1", par), "pe")
    xn1T_ready = P.at("dv")

    otstore_ms = {}
    H_RUN = H if h_lim is None else h_lim
    for h in (range(H_RUN) if "1" in phases else []):
        par = h % 2
        wld = []
        for wsb, wr in ((qwsb, qwr), (kwsb, kwr), (vwsb, vwr)):
            wld.append(P.dma("sync", ("qw0", "qw1")[par], wsb[par][:],
                             wr[h].rearrange("p c j -> p (c j)"),
                             wait=[wfree(buffree, ("hw", par)), attn_rt]))
        qkv_dst = (qt_h, kt_h, vt_h)
        for wi, (wsb, dstT) in (list(enumerate(zip((qwsb, kwsb, vwsb), qkv_dst)))
                                if "q" in p1_parts else []):
            for tokg in range(S // 512):
                qi = tokg % 2
                bank, bkey = puq(qi), f"puq{qi}"
                for c in range(EC):
                    P.op("tensor", lambda E_, b=bank, w=wsb[par], c=c, tokg=tokg: E_.matmul(
                        b, w[:, c * 128:(c + 1) * 128],
                        xn1T[:, c * S + tokg * 512: c * S + tokg * 512 + 512],
                        start=(c == 0), stop=(c == EC - 1)),
                         wait=([wld[wi], xn1T_ready, wfree(bankfree, bkey)]
                               if c == 0 else []),
                         inc=([P.tick("pe")] if c == EC - 1 else []))
                mm_ms = pe_flush(bank[0:1, 0:512])
                P.op("vector", lambda E_, b=bank, d=dstT, tokg=tokg: E_.tensor_copy(
                    d[:, tokg * 512:(tokg + 1) * 512], b),
                     wait=[mm_ms])
                free(bankfree, bkey, "dv")
        qkv_done = P.at("dv")
        free(buffree, ("hw", par), "pe")
        vtev = {}
        pend = None
        nvb = S // 512

        def vt_evac(kb, half, tick):
            P.op("vector", lambda E_, half=half, kb=kb: E_.tensor_copy(
                v_sb[:, kb * 512:(kb + 1) * 512],
                psb[:, half * 1024: half * 1024 + 512]),
                 wait=[tick])
            vtev[kb] = P.at("dv")
            free(bankfree, ("psbA", "psbB")[half], "dv")

        for kb in (range(nvb) if "v" in p1_parts else []):
            half = kb % 2
            bkey = ("psbA", "psbB")[half]
            for k in range(4):
                kc = kb * 4 + k
                P.op("tensor", lambda E_, half=half, k=k, kc=kc: E_.transpose(
                    psb[:, half * 1024 + k * 128: half * 1024 + (k + 1) * 128],
                    vt_h[:, kc * 128:(kc + 1) * 128], id16[:]),
                     wait=([qkv_done, consts_ready, wfree(bankfree, bkey)]
                           if k == 0 else []),
                     inc=([P.tick("pe")] if k == 3 else []))
            tick = P.at("pe")
            if pend is not None:
                vt_evac(pend[0], pend[1], tick)
            pend = (kb, half)
        if pend is not None:
            dh = (pend[1] + 1) % 2
            P.op("tensor", lambda E_, dh=dh: E_.transpose(
                psb[:, dh * 1024: dh * 1024 + 128], vt_h[:, 0:128], id16[:]),
                 wait=[wfree(bankfree, ("psbA", "psbB")[dh])])
            P.op("tensor", lambda E_, dh=dh: E_.transpose(
                psb[:, dh * 1024 + 128: dh * 1024 + 256], vt_h[:, 0:128], id16[:]),
                 inc=[P.tick("pe")])
            vt_evac(pend[0], pend[1], P.at("pe"))
        vsb_ready = P.at("dv")

        for qg in (range(NQG if qg_lim is None else qg_lim)
                   if "a" in p1_parts else []):
            nkc = 4 * qg + 4

            def sigma_pv(st):
                kc, sp, loc, nv, exp_ms = st
                first = (kc == 0)
                last = (kc == nkc - 1)
                P.op("tensor", lambda E_, loc=loc, nv=nv, sp=sp, f=first, l=last: E_.matmul(
                    ps4[:, loc:512], ones16[:], ptb[sp][:, 0:nv], start=f, stop=l),
                     wait=[exp_ms, consts_ready]
                     + ([wfree(bankfree, "ps4"), wfree(bankfree, "ps5")] if first else []))
                P.op("tensor", lambda E_, loc=loc, nv=nv, sp=sp, kc=kc, f=first, l=last: E_.matmul(
                    ps5[:, loc:512], v_sb[:, kc * 128:(kc + 1) * 128], ptb[sp][:, 0:nv],
                    start=f, stop=l),
                     wait=[vtev.get(kc // 4)], inc=[P.tick("pe")])
                free(buffree, ("ptb", kc % 2), "pe")

            pend_sp = None
            for kc in range(nkc):
                loc = max(0, kc * 128 - qg * 512)
                nv = 512 - loc
                spar = kc % 2
                sbank, skey = puq(2 + spar), f"puq{2 + spar}"
                P.op("tensor", lambda E_, b=sbank, loc=loc, kc=kc, qg=qg: E_.matmul(
                    b[:, loc:512], kt_h[:, kc * 128:(kc + 1) * 128],
                    qt_h[:, qg * 512 + loc: (qg + 1) * 512],
                    start=True, stop=True),
                     wait=[qkv_done, wfree(bankfree, skey)],
                     inc=[P.tick("pe")])
                stmm_ms = pe_flush(sbank[0:1, loc:512])
                mask_ms = None
                if kc * 128 >= qg * 512:
                    P.op("vector", lambda E_, b=sbank, loc=loc: E_.tensor_add(
                        b[:, loc:loc + 128], b[:, loc:loc + 128], c_mask),
                         wait=[stmm_ms])
                    mask_ms = P.at("dv")
                P.op("scalar", lambda E_, b=sbank, loc=loc, nv=nv, sp=spar: E_.activation(
                    ptb[sp][:, 0:nv], b[:, loc:512], AF.Exp, bias=c_negC, scale=c_isd),
                     wait=[stmm_ms, mask_ms, wfree(buffree, ("ptb", spar))])
                exp_ms = P.at("ac")
                free(bankfree, skey, "ac")
                if pend_sp is not None:
                    sigma_pv(pend_sp)
                pend_sp = (kc, spar, loc, nv, exp_ms)
            sigma_pv(pend_sp)
            pe_flush(ps4[0:1, 0:512])
            pv_ms = pe_flush(ps5[0:1, 0:512])
            opar = qg % 2
            P.op("vector", lambda E_, o=opar: E_.reciprocal(invs[o][:], ps4[:]),
                 wait=[pv_ms])
            free(bankfree, "ps4", "dv")
            P.op("vector", lambda E_, o=opar: E_.tensor_tensor(
                otst[o][:], ps5[:], invs[o][:], ALU.mult),
                 wait=[P.at("dv"), wfree(buffree, ("otst", opar))])
            ot_ms = P.at("dv")
            free(bankfree, "ps5", "dv")
            otstore_ms[(h, qg)] = P.dma(
                "sync", ("ot0", "ot1")[opar], ots[h][:, qg * 512:(qg + 1) * 512],
                otst[opar][:], wait=[ot_ms])
            free(buffree, ("otst", opar), ("ot0", "ot1")[opar])
            bankfree["ps4"] = otstore_ms[(h, qg)]
            bankfree["ps5"] = otstore_ms[(h, qg)]
            P.barrier()
            if dbg_attn is not None and h == dbg_attn[0] and qg == NQG - 1:
                dqt = nc.dram_tensor("dbg_qt", [128, 4 * S], BF16,
                                     kind="ExternalOutput").ap()
                P.dma("sync", "dbgw", dqt[:, 0:S], qt_h[:], wait=[ot_ms])
                P.dma("sync", "dbgw", dqt[:, S:2 * S], kt_h[:])
                P.dma("sync", "dbgw", dqt[:, 2 * S:3 * S], vt_h[:])
                P.dma("sync", "dbgw", dqt[:, 3 * S:4 * S], v_sb[:])

    owld = None
    for c in (range(EC) if "2" in phases else []):
        owld = P.dma("sync", "p2c", owsb[:, c * E:(c + 1) * E],
                     owb[c * 128:(c + 1) * 128, :], wait=[ow_rt])
    gwld = None
    if "2" in phases:
        gwld = P.dma("sync", "p2c",
                     gw_sb.ap().rearrange("p (c n) -> p c n", n=NEXP),
                     gw_in.rearrange("(c p) n -> p c n", p=128))
    ttflush = {}
    x1store_ms = {}
    owld = P.at("p2c")
    gwld = P.at("p2c")
    ot_final0, ot_final1 = P.at("ot0"), P.at("ot1")
    for g2 in (range(NQG) if "2" in phases else []):
        opar = g2 % 2
        otld = None
        for c in range(EC):
            w = [ot_final0, ot_final1]
            if c == 0:
                w.append(wfree(buffree, ("otsb", opar)))
            otld = P.dma("sync", ("ob0", "ob1")[opar],
                         otsb[opar][:, c * 512:(c + 1) * 512],
                         ots[c][:, g2 * 512:(g2 + 1) * 512], wait=w)
        for tt in range(4):
            t = g2 * 4 + tt
            par = t % 2
            ldx2 = P.dma("sync", ("xa", "xb")[par], xld2[par][:],
                         x_in[t * 128:(t + 1) * 128, :],
                         wait=[wfree(buffree, ("xld2", par))])
            for c in range(EC):
                for j in range(NQ):
                    P.op("tensor", lambda E_, c=c, j=j, o=opar, tt=tt: E_.matmul(
                        puq(j), otsb[o][:, c * 512 + tt * 128: c * 512 + (tt + 1) * 128],
                        owsb[:, c * E + j * 512: c * E + (j + 1) * 512],
                        start=(c == 0), stop=(c == EC - 1)),
                         wait=([otld, owld] + [wfree(bankfree, f"puq{jj}")
                                               for jj in range(NQ)]
                               if (c == 0 and j == 0) else []),
                         inc=([P.tick("pe")] if (c == EC - 1 and j == NQ - 1)
                              else []))
            for j in range(NQ):
                u_ms = pe_flush(pu[0:1, j * 512:(j + 1) * 512])
            a = aux[par]
            ln_chain(a, [pu[:, q2 * 512:(q2 + 1) * 512] for q2 in range(NQ)],
                 wait=[u_ms, wfree(buffree, ("aux", par))])
            P.op("scalar", lambda E_, a=a: E_.activation(
                uln[:], pu[:, 0:E], AF.Identity, scale=a[:, 27:28], bias=a[:, 28:29]),
                 wait=[P.at("dv"), wfree(buffree, "uln")])
            for j in range(NQ):
                free(bankfree, f"puq{j}", "ac")
            P.op("vector", lambda E_, par=par: E_.tensor_add(
                x1st[par][:], uln[:], xld2[par][:]),
                 wait=[P.at("ac"), ldx2, wfree(buffree, ("x1st", par))])
            x1_ms = P.at("dv")
            free(buffree, ("xld2", par), "dv")
            free(buffree, "uln", "dv")
            x1store_ms[t] = P.dma("sync", ("x1a", "x1b")[par],
                                  x1s[t * 128:(t + 1) * 128, :],
                                  x1st[par][:], wait=[x1_ms])
            ln_chain(a, [x1st[par][:, q2 * 512:(q2 + 1) * 512] for q2 in range(NQ)],
                 mcol=56, scol=32, wait=[x1_ms])
            P.op("scalar", lambda E_, a=a, par=par: E_.activation(
                t32[:], x1st[par][:], AF.Identity, scale=a[:, 59:60], bias=a[:, 60:61]),
                 wait=[P.at("dv"), wfree(buffree, "t32")])
            t32_ms = P.at("ac")
            free(buffree, ("aux", par), "ac")
            free(buffree, ("x1st", par), "ac")
            for g4 in range(EC // 4):
                bank = (ps4, ps5)[g4 % 2]
                bkey = ("ps4", "ps5")[g4 % 2]
                for k in range(4):
                    c = g4 * 4 + k
                    P.op("tensor", lambda E_, b=bank, k=k, c=c: E_.transpose(
                        b[:, k * 128:(k + 1) * 128], t32[:, c * 128:(c + 1) * 128], c_id),
                         wait=([t32_ms, wfree(bankfree, bkey),
                                wfree(buffree, "t32T")] if k == 0 else []),
                         inc=([P.tick("pe")] if k == 3 else []))
                pe_flush(bank[0:1, 384:512])
                tp_ms = pe_flush(bank[0:1, 384:512])
                P.op("vector", lambda E_, b=bank, g4=g4: E_.tensor_copy(
                    t32T[:, g4 * 512:(g4 + 1) * 512], b[:]), wait=[tp_ms])
                def evt(E_, b=bank, g4=g4, tt=tt):
                    src = b.ap().rearrange("p (c k) -> p c k", k=128)[:, 0:4, :]
                    dst = ttst.ap().rearrange("p (c s) -> p c s", c=EC)[
                        :, g4 * 4:g4 * 4 + 4, tt * 128:(tt + 1) * 128]
                    return E_.tensor_copy(dst, src)
                P.op("vector", evt, wait=[wfree(buffree, "ttst")])
                free(bankfree, bkey, "dv")
            t32T_ms = P.at("dv")
            free(buffree, "t32", "pe")
            for c in range(EC):
                P.op("tensor", lambda E_, c=c: E_.matmul(
                    pu[:, 0:NEXP], t32T[:, c * 128:(c + 1) * 128],
                    gw_sb[:, c * NEXP:(c + 1) * NEXP],
                    start=(c == 0), stop=(c == EC - 1)),
                     wait=([t32T_ms, gwld, wfree(bankfree, "puq0")] if c == 0 else []),
                     inc=([P.tick("pe")] if c == EC - 1 else []))
            for _f in range(8):
                g_ms = pe_flush(pu[0:1, 0:NEXP])
            free(buffree, "t32T", "pe")
            P.op("vector", lambda E_: E_.reduce_max(gaux[:, 0:1], pu[:, 0:NEXP],
                                                    axis=AX.X), wait=[g_ms])
            P.op("vector", lambda E_: E_.tensor_scalar(
                gaux[:, 1:2], gaux[:, 0:1], -1.0, None, op0=ALU.mult),
                 wait=[P.at("dv")])
            P.op("scalar", lambda E_: E_.activation(
                ge_sb[:], pu[:, 0:NEXP], AF.Exp, bias=gaux[:, 1:2],
                accum_out=gaux[:, 2:3]),
                 wait=[P.at("dv")])
            free(bankfree, "puq0", "ac")
            P.op("vector", lambda E_: E_.reciprocal(gaux[:, 3:4], gaux[:, 2:3]),
                 wait=[P.at("ac")])
            P.op("vector", lambda E_: E_.tensor_scalar(
                gp_sb[:], ge_sb[:], gaux[:, 3:4], None, op0=ALU.mult),
                 wait=[P.at("dv")])
            P.op("vector", lambda E_: E_.reduce_max(gaux[:, 4:5], gp_sb[:], axis=AX.X),
                 wait=[P.at("dv")])
            P.op("vector", lambda E_: E_.tensor_scalar(
                gi_sb[:], gp_sb[:], gaux[:, 4:5], None, op0=ALU.is_equal),
                 wait=[P.at("dv")])
            P.op("vector", lambda E_: E_.scalar_tensor_tensor(
                gj_sb[:], gi_sb[:], -10.0, gp_sb[:], op0=ALU.mult, op1=ALU.add),
                 wait=[P.at("dv")])
            P.op("vector", lambda E_: E_.reduce_max(gaux[:, 5:6], gj_sb[:], axis=AX.X),
                 wait=[P.at("dv")])
            P.op("vector", lambda E_: E_.tensor_scalar(
                gi_sb[:], gp_sb[:], gaux[:, 5:6], None, op0=ALU.is_ge),
                 wait=[P.at("dv")])
            P.op("vector", lambda E_, t=t: E_.tensor_tensor(
                wg[:, t * NEXP:(t + 1) * NEXP], gp_sb[:], gi_sb[:], ALU.mult),
                 wait=[P.at("dv")])
        free(buffree, ("otsb", opar), "pe")
        fl = None
        for c in range(EC):
            fl = P.dma("sync", "ttf", tts[c][:, g2 * 512:(g2 + 1) * 512],
                       ttst[:, c * 512:(c + 1) * 512], wait=[P.at("dv")])
        ttflush[g2] = fl
        free(buffree, "ttst", "ttf")
    wg_done = P.at("dv")

    ttf_final = P.at("ttf")
    x1s_final = [P.at("x1a"), P.at("x1b")]
    for m in (range(NQG) if "3" in phases else []):
        ttld = None
        for c in range(EC):
            w = [ttf_final]
            if c == 0:
                w.append(wfree(buffree, "ttsb"))
            ttld = P.dma("sync", "ttl", ttsb[:, c * 512:(c + 1) * 512],
                         tts[c][:, m * 512:(m + 1) * 512], wait=w)

        def ab_phase(nic, tokn, tok0, wsrc1, wsrc3, hcols, rt_wait):
            for ic in range(nic):
                wpar = ic % 2
                l1 = P.dma("sync", ("w1a", "w1b")[wpar], w1sb[wpar][:], wsrc1(ic),
                           wait=[wfree(buffree, ("w1sb", wpar)), rt_wait])
                l3 = P.dma("sync", ("w3a", "w3b")[wpar], w3sb[wpar][:], wsrc3(ic),
                           wait=[wfree(buffree, ("w3sb", wpar))])
                ppar = ic % 2
                abank, akey = puq(ppar * 2), f"puq{ppar * 2}"
                bbank, bkey = puq(ppar * 2 + 1), f"puq{ppar * 2 + 1}"
                for dc in range(EC):
                    P.op("tensor", lambda E_, b=abank, w=w1sb[wpar], dc=dc,
                         tok0=tok0, tokn=tokn: E_.matmul(
                        b[:, 0:tokn], w[:, dc * 128:(dc + 1) * 128],
                        ttsb[:, dc * 512 + tok0: dc * 512 + tok0 + tokn],
                        start=(dc == 0), stop=(dc == EC - 1)),
                         wait=([l1, l3, ttld, wfree(bankfree, akey),
                                wfree(bankfree, bkey)] if dc == 0 else []))
                    P.op("tensor", lambda E_, b=bbank, w=w3sb[wpar], dc=dc,
                         tok0=tok0, tokn=tokn: E_.matmul(
                        b[:, 0:tokn], w[:, dc * 128:(dc + 1) * 128],
                        ttsb[:, dc * 512 + tok0: dc * 512 + tok0 + tokn],
                        start=(dc == 0), stop=(dc == EC - 1)),
                         inc=([P.tick("pe")] if dc == EC - 1 else []))
                ab_ms = pe_flush(bbank[0:1, 0:tokn])
                free(buffree, ("w1sb", wpar), "pe")
                free(buffree, ("w3sb", wpar), "pe")
                P.op("scalar", lambda E_, b=abank, tokn=tokn: E_.activation(
                    sa_st[:, 0:tokn], b[:, 0:tokn], AF.Silu),
                     wait=[ab_ms, wfree(buffree, "sast")])
                sil_ms = P.at("ac")
                free(bankfree, akey, "ac")
                P.op("vector", lambda E_, b=bbank, tokn=tokn: E_.tensor_copy(
                    sb_st[:, 0:tokn], b[:, 0:tokn]), wait=[ab_ms])
                P.op("vector", lambda E_, ic=ic, tokn=tokn, hcols=hcols: E_.tensor_tensor(
                    hT[:, ic * hcols: ic * hcols + tokn], sa_st[:, 0:tokn],
                    sb_st[:, 0:tokn], ALU.mult),
                     wait=[sil_ms, P.at("dv")])
                free(bankfree, bkey, "dv")
                free(buffree, "sast", "dv")
            return P.at("dv")

        def m3_phase(nic, ndq, dn, tokn, tok0, hcols, w2src, we_col, rt_wait):
            h_ms = P.at("dv")
            ntokt = tokn // 128
            for dq in range(ndq):
                wpar = dq % 2
                l2 = P.dma("sync", ("w2a", "w2c")[wpar],
                           w2q[wpar][:, 0:nic * dn].rearrange("p (ic d) -> p ic d", d=dn),
                           w2src(dq), wait=[wfree(buffree, ("w2q", wpar)), rt_wait])
                for tokt in range(ntokt):
                    ypar = (dq * ntokt + tokt) % 2
                    ybank, ykey = (ps4, ps5)[ypar], ("ps4", "ps5")[ypar]
                    for ic in range(nic):
                        P.op("tensor", lambda E_, yb=ybank, ic=ic, tokt=tokt, dn=dn,
                             w=w2q[wpar], hcols=hcols: E_.matmul(
                            yb[:, 0:dn],
                            hT[:, ic * hcols + tokt * 128: ic * hcols + (tokt + 1) * 128],
                            w[:, ic * dn:(ic + 1) * dn],
                            start=(ic == 0), stop=(ic == nic - 1)),
                             wait=([h_ms, l2, wfree(bankfree, ykey)] if ic == 0 else []),
                             inc=([P.tick("pe")] if ic == nic - 1 else []))
                    y_ms = pe_flush(ybank[0:1, 0:dn])
                    gtok = (tok0 + tokt * 128) // 128
                    ycol = gtok * E + dq * dn
                    if we_col is None:
                        P.op("vector", lambda E_, yb=ybank, ycol=ycol, dn=dn: E_.tensor_copy(
                            yacc[:, ycol:ycol + dn], yb[:, 0:dn]),
                             wait=[y_ms, wfree(buffree, "yacc")])
                    else:
                        gt = m * 4 + gtok
                        P.op("vector", lambda E_, yb=ybank, ycol=ycol, dn=dn, gt=gt,
                             e=we_col: E_.scalar_tensor_tensor(
                            yacc[:, ycol:ycol + dn], yb[:, 0:dn],
                            wg[:, gt * NEXP + e: gt * NEXP + e + 1],
                            yacc[:, ycol:ycol + dn], op0=ALU.mult, op1=ALU.add),
                             wait=[y_ms])
                    free(bankfree, ykey, "dv")
                free(buffree, ("w2q", wpar), "pe")

        sw2r_ap = sw2b.rearrange("(ic p) (dq d) -> p ic dq d", p=128, d=256)
        for sub in range(2):
            tok0 = sub * 256
            ab_phase(SIC, 256, tok0,
                     lambda ic: sw1r[ic].rearrange("p c j -> p (c j)"),
                     lambda ic: sw3r[ic].rearrange("p c j -> p (c j)"), 256, sh_rt)
            m3_phase(SIC, E // 256, 256, 256, tok0, 256,
                     lambda dq: sw2r_ap[:, :, dq], None, sh_rt)
        for e in range(NEXP):
            ab_phase(IC, 512, 0,
                     lambda ic, e=e: w1r[e, ic].rearrange("p c j -> p (c j)"),
                     lambda ic, e=e: w3r[e, ic].rearrange("p c j -> p (c j)"), 512,
                     exp_rt[e])
            w2r_ap = w2b[e].rearrange("(ic p) (dq d) -> p ic dq d", p=128, d=512)
            m3_phase(IC, E // 512, 512, 512, 0, 512,
                     lambda dq, a2=w2r_ap: a2[:, :, dq], e, exp_rt[e])
        moe_done = P.at("dv")
        free(buffree, "ttsb", "pe")
        for tokt in range(4):
            t = m * 4 + tokt
            par = tokt % 2
            ldx1 = P.dma("sync", ("xra", "xrb")[par], x1ld[par][:],
                         x1s[t * 128:(t + 1) * 128, :],
                         wait=[wfree(buffree, ("x1ld", par))] + x1s_final)
            a = aux[par]
            ln_chain(a, [yacc[:, tokt * E + q2 * 512: tokt * E + (q2 + 1) * 512]
                         for q2 in range(NQ)],
                 wait=[moe_done, wfree(buffree, ("aux", par))])
            P.op("scalar", lambda E_, a=a, tokt=tokt: E_.activation(
                epiln[:], yacc[:, tokt * E:(tokt + 1) * E], AF.Identity,
                scale=a[:, 27:28], bias=a[:, 28:29]),
                 wait=[P.at("dv"), wfree(buffree, "epiln")])
            free(buffree, ("aux", par), "ac")
            P.op("vector", lambda E_, par=par: E_.tensor_add(
                outst[par][:], epiln[:], x1ld[par][:]),
                 wait=[P.at("ac"), ldx1, wfree(buffree, ("outst", par))])
            free(buffree, ("x1ld", par), "dv")
            free(buffree, "epiln", "dv")
            P.dma("sync", ("oua", "oub")[par], out_t[t * 128:(t + 1) * 128, :],
                  outst[par][:], wait=[P.at("dv")])
            free(buffree, ("outst", par), ("oua", "oub")[par])
        free(buffree, "yacc", "ac")

    if debug and "2" in phases:
        wgd = nc.dram_tensor("wgdbg", [128, NT * NEXP], F32,
                             kind="ExternalOutput").ap()
        P.dma("sync", "dbgw", wgd, wg[:], wait=[wg_done])
    if debug and phases == "0":
        xbg = nc.dram_tensor("xn1Tdbg", [128, EC * S], BF16,
                             kind="ExternalOutput").ap()
        P.dma("sync", "dbgw", xbg, xn1T[:], wait=[xn1T_ready])
        axd = nc.dram_tensor("auxdbg", [128, 128], F32,
                             kind="ExternalOutput").ap()
        P.dma("sync", "dbgw", axd[:, 0:64], aux[0][:], wait=[xn1T_ready])
        P.dma("sync", "dbgw", axd[:, 64:128], aux[1][:], wait=[xn1T_ready])

    for _e in Plan.ENGS:
        P.c.setdefault("ch_" + _e, 0)
    dma_sems_sync = [s for s in P.c
                     if s not in ("pe", "dv", "ac", "po")
                     and not s.startswith("r") and not s.startswith("ch_")]
    dma_sems_gq = [s for s in P.c if s.startswith("r") and s != "rqk"] + ["rqk"]
    dma_sems_gq = [s for s in dma_sems_gq if s in P.c]
    finals_sync = [(s, P.c[s]) for s in dma_sems_sync]
    finals_gq = [(s, P.c[s]) for s in dma_sems_gq]

    from contextlib import ExitStack
    es = ExitStack()
    sem_handles = {}
    for s in sorted(P.c):
        sem_handles[s] = es.enter_context(nc.semaphore(s))
    P._bar = sem_handles.get("bar")

    with es, nc.Block() as block:
        def mk(eng_name):
            ch = sem_handles["ch_" + eng_name]

            def emitfn(E_):
                cnt = 0
                guard = (eng_name == "tensor")
                for w, fn, incs in P.q[eng_name]:
                    if len(w) > 1:
                        for (s, v) in w:
                            E_.wait_ge(sem_handles[s], v).then_inc(ch, 1)
                            cnt += 1
                        if guard:
                            E_.wait_ge(ch, cnt)
                    elif len(w) == 1 and guard:
                        E_.wait_ge(sem_handles[w[0][0]], w[0][1])
                    r = fn(E_)
                    if r is None:
                        assert not incs and not w
                        continue
                    first, last = (r if isinstance(r, tuple) else (r, r))
                    if len(w) == 1:
                        first._wait_ge(sem_handles[w[0][0]], w[0][1])
                    elif len(w) > 1:
                        first._wait_ge(ch, cnt)
                    for (s, n) in incs:
                        last.then_inc(sem_handles[s], n)
                fin = (finals_sync if eng_name == "sync"
                       else finals_gq if eng_name == "gpsimd" else [])
                for s, v in fin:
                    if v > 0:
                        E_.wait_ge(sem_handles[s], v).then_inc(ch, 1)
                        cnt += 1
                if fin:
                    E_.wait_ge(ch, cnt)
            return emitfn

        block.sync(mk("sync"))
        block.gpsimd(mk("gpsimd"))
        block.tensor(mk("tensor"))
        block.vector(mk("vector"))
        block.scalar(mk("scalar"))
    return nc


def make_consts(cfg):
    cst = np.zeros((128, 512), np.float32)
    cst[:, 0:128] = np.eye(128, dtype=np.float32)
    cst[:, 128:256] = np.triu(np.full((128, 128), MASK_NEG, np.float32), 1).T
    cst[:, 256:384] = 1.0
    cst[:, 384] = LN_EPS
    cst[:, 385] = -0.5
    cst[:, 386] = 1.0 / np.sqrt(128.0)
    cst[:, 387] = -SOFT_C
    return cst


def make_in_map(cfg, x_core, qw, kw, vw, ow, gate_w, w1, w2, w3, sw1, sw2, sw3):
    c = np.ascontiguousarray
    return {
        "x": c(np.asarray(x_core, np.float32)),
        "qw": c(np.asarray(qw, np.float32)),
        "kw": c(np.asarray(kw, np.float32)),
        "vw": c(np.asarray(vw, np.float32)),
        "ow": c(np.asarray(ow, np.float32)),
        "gw": c(np.asarray(gate_w, np.float32).T),
        "w1t": c(np.asarray(w1, np.float32).transpose(0, 2, 1)),
        "w3t": c(np.asarray(w3, np.float32).transpose(0, 2, 1)),
        "w2t": c(np.asarray(w2, np.float32).transpose(0, 2, 1)),
        "sw1t": c(np.asarray(sw1, np.float32).T),
        "sw3t": c(np.asarray(sw3, np.float32).T),
        "sw2t": c(np.asarray(sw2, np.float32).T),
        "cst": make_consts(cfg),
    }


def kernel(L0_attn_hidden_states, L0_attn_q_wei, L0_attn_k_wei, L0_attn_v_wei,
           L0_attn_out_wei, L0_ffn_gate_w, L0_ffn_w1, L0_ffn_w2, L0_ffn_w3,
           L0_ffn_sw1, L0_ffn_sw2, L0_ffn_sw3):
    cfg = CFG_FULL
    x = np.asarray(L0_attn_hidden_states, np.float32)
    n = cfg["n_cores"]
    base = make_in_map(cfg, x[0], L0_attn_q_wei, L0_attn_k_wei, L0_attn_v_wei,
                       L0_attn_out_wei, L0_ffn_gate_w, L0_ffn_w1, L0_ffn_w2,
                       L0_ffn_w3, L0_ffn_sw1, L0_ffn_sw2, L0_ffn_sw3)
    in_maps = []
    for i in range(n):
        mp = dict(base)
        mp["x"] = np.ascontiguousarray(x[i])
        in_maps.append(mp)
    nc = build(cfg)
    res = run_bass_kernel_spmd(nc, in_maps, core_ids=list(range(n)))
    return np.stack([np.asarray(r["out"], np.float32) for r in res.results], 0)
